# version 01
# baseline (speedup 1.0000x reference)
import math
from contextlib import ExitStack

import numpy as np
import ml_dtypes

import concourse.bass as bass
import concourse.mybir as mybir
from concourse.ap import AP
from concourse.bass_utils import run_bass_kernel_spmd

F32 = mybir.dt.float32
BF16 = mybir.dt.bfloat16
AF = mybir.ActivationFunctionType
ALU = mybir.AluOpType
BF = ml_dtypes.bfloat16

B, N, D = 2, 8192, 2048
HEADS, HDIM = 16, 128
CK = 128
DFF = 8192
NCORE = 8
NPC = N // NCORE
TPC = B * NPC
FC = D // 128
HC = DFF // 128
JPC = NPC // CK
LOG_LAM = [-(2.0 ** (-8.0 * (h + 1) / HEADS)) for h in range(HEADS)]
EPS_LN = 1e-5
EPS_RMS = 1e-6
FS = 512
DEBUG_ATTN0 = False
UPTO = 6

ENGINES = ("pe", "act", "dve", "pool", "sp")
QUEUES = ("qsp", "qact", "qpool")
STREAMS = ENGINES + QUEUES
Q_ISSUER = {"qsp": "sp", "qact": "act", "qpool": "pool"}


def _hadamard(n):
    h = np.array([[1.0]], dtype=np.float64)
    while h.shape[0] < n:
        h = np.block([[h, h], [h, -h]])
    return h


def host_consts(core):
    c = {}
    n0 = core * NPC
    pos = np.arange(n0, n0 + NPC, dtype=np.float64)
    half = HDIM // 2
    theta = 10000.0 ** (-2.0 * np.arange(half, dtype=np.float64) / HDIM)
    ang = pos[None, :] * theta[:, None]
    cosv, sinv = np.cos(ang), np.sin(ang)
    cc = np.concatenate([cosv, cosv], axis=0)
    cc = np.concatenate([cc, cc], axis=1)
    ss = np.concatenate([-sinv, sinv], axis=0)
    ss = np.concatenate([ss, ss], axis=1)
    c["rope_cc"] = np.ascontiguousarray(cc.astype(BF))
    c["rope_ss"] = np.ascontiguousarray(ss.astype(BF))
    w = np.zeros((NCORE, HEADS), dtype=np.float64)
    for m in range(core):
        for h in range(HEADS):
            w[m, h] = math.exp(LOG_LAM[h] * NPC * (core - 1 - m))
    c["sprefw"] = np.ascontiguousarray(np.broadcast_to(
        w.reshape(1, NCORE * HEADS), (128, NCORE * HEADS)).astype(np.float32))
    return c


def shared_consts():
    c = {}
    i = np.arange(CK, dtype=np.float64)
    diff = i[:, None] - i[None, :]
    dmask_t, dq, dk = [], [], []
    for h in range(HEADS):
        ll = LOG_LAM[h]
        m = np.where(diff >= 0, np.exp(ll * diff), 0.0)
        dmask_t.append(m.T.copy())
        dq.append(np.broadcast_to(np.exp(ll * (i + 1.0)), (128, CK)).copy())
        dk.append(np.broadcast_to(np.exp(ll * (CK - 1.0 - i)), (128, CK)).copy())
    c["dmask_t"] = np.stack(dmask_t).astype(BF)
    c["dq"] = np.stack(dq).astype(BF)
    c["dk"] = np.stack(dk).astype(BF)
    c["h128"] = _hadamard(128).astype(BF)
    c["ones_col"] = np.ones((128, 1), dtype=BF)
    c["ones_row"] = np.ones((1, 128), dtype=BF)
    p = np.zeros((128, 128), dtype=BF)
    p[np.arange(64), np.arange(64, 128)] = 1.0
    p[np.arange(64, 128), np.arange(64)] = 1.0
    c["swap"] = p
    c["ident"] = np.eye(128, dtype=BF)
    return c


NPOOL = 56


class Op:
    __slots__ = ("token", "fn", "waits", "tag", "is_dma")

    def __init__(self, token, fn, tag, is_dma):
        self.token = token
        self.fn = fn
        self.waits = []
        self.tag = tag
        self.is_dma = is_dma


class _Bump:
    def __init__(self, cols, name):
        self.cols = cols
        self.name = name
        self.free_l = [(0, cols, ())]
        self.hw = 0

    def alloc(self, n, tag):
        n = (n + 15) // 16 * 16
        for i, (off, sz, guards) in enumerate(self.free_l):
            if sz >= n:
                rest = (off + n, sz - n, guards)
                self.free_l[i] = rest
                if rest[1] == 0:
                    self.free_l.pop(i)
                self.hw = max(self.hw, off + n)
                return off, guards
        raise MemoryError(
            f"sbuf arena {self.name} out of space for {tag} (need {n}, "
            f"free={[(o, s) for o, s, _ in self.free_l]})")

    def release(self, off, n, guard):
        n = (n + 15) // 16 * 16
        self.free_l.append((off, n, (guard,)))
        self.free_l.sort(key=lambda x: x[0])
        merged = []
        for o, s, g in self.free_l:
            if merged and merged[-1][0] + merged[-1][1] == o:
                po, ps, pg = merged[-1]
                merged[-1] = (po, ps + s, tuple(set(pg) | set(g)))
            else:
                merged.append((o, s, g))
        self.free_l = merged


class _Tile:
    _serial = [0]

    def __init__(self, prog, arena, alloc, cols, tag):
        self.prog = prog
        self.arena = arena
        self.cols = cols
        self._alloc = alloc
        off, prior = alloc.alloc(cols, tag)
        self.off = off
        _Tile._serial[0] += 1
        self.key = ("T", _Tile._serial[0], tag)
        if prior:
            prog.pending_hazard[self.key] = prior
        self.freed = False

    def ap(self, rows=128, c0=0, c1=None):
        c1 = self.cols if c1 is None else c1
        return self.arena[0:rows, self.off + c0:self.off + c1]

    def ap3(self, shape, rows=128, c0=0):
        n1, n2 = shape
        full = self.ap()
        tot = full.ap[0][0]
        return AP(full.tensor, self.off + c0,
                  [[tot, rows], [n2, n1], [1, n2]])

    def free(self):
        assert not self.freed
        self.freed = True
        self._alloc.release(self.off, self.cols, self.key)


class Prog:
    def __init__(self, name):
        self.nc = bass.Bass("TRN2", target_bir_lowering=False)
        self.stack = ExitStack()
        self.ops_by_eng = {e: [] for e in ENGINES}
        self.ecnt = {e: 0 for e in ENGINES}
        self.dma_rr = 0
        self.dma_uses = [0] * NPOOL
        self.last_w = {}
        self.readers = {}
        self.frontier = {e: {} for e in ENGINES}
        self.snap = {}
        self.pending_hazard = {}
        self.paranoid = False
        self.psum = {}
        self.consts = {}
        self.out_queues = set()
        self.dram_in = {}
        self.dram_out = {}

    def alloc_arenas(self, bf_cols, f32_cols):
        self.ab = self.stack.enter_context(
            self.nc.sbuf_tensor("arena_bf", [128, bf_cols], BF16))
        self.af = self.stack.enter_context(
            self.nc.sbuf_tensor("arena_f", [128, f32_cols], F32))
        self._ab_alloc = _Bump(bf_cols, "bf16")
        self._af_alloc = _Bump(f32_cols, "f32")

    def tile_b(self, cols, tag=""):
        return _Tile(self, self.ab, self._ab_alloc, cols, tag)

    def tile_f(self, cols, tag=""):
        return _Tile(self, self.af, self._af_alloc, cols, tag)

    def alloc_psum(self):
        self.psum["P0"] = self.stack.enter_context(
            self.nc.psum_tensor("P0", [128, 2048], F32))
        self.psum["P1"] = self.stack.enter_context(
            self.nc.psum_tensor("P1", [128, 1024], F32))
        self.psum["P2"] = self.stack.enter_context(
            self.nc.psum_tensor("P2", [128, 512], F32))
        self.psum["PB"] = self.stack.enter_context(
            self.nc.psum_tensor("PB", [128, 1024], BF16))

    def load_consts(self, sc, names, extra=()):
        nc = self.nc
        defs = {}
        for nm in names:
            if nm in ("dmask_t", "dq", "dk"):
                for h in range(HEADS):
                    defs[f"{nm}{h}"] = sc[nm][h]
            else:
                defs[nm] = sc[nm]
        for nm in extra:
            defs[nm] = None
        self.const_tiles = {}
        for nm, arr in defs.items():
            if arr is not None:
                dt = nc.inline_tensor(np.ascontiguousarray(arr), f"c_{nm}")
                src = dt.ap()
                rows, cols = arr.shape
                is_f32 = False
            else:
                src = self.dram_in[nm]
                rows, cols = src.shape
                is_f32 = (src.tensor.dtype == F32)
            t = self.tile_f(cols, nm) if is_f32 else self.tile_b(cols, nm)
            self.dma("qsp", lambda t=t, src=src, rows=rows:
                     self.nc.sync.dma_start(out=t.ap(rows=rows), in_=src),
                     writes=[t], tag=f"ld_{nm}")
            self.const_tiles[nm] = (t, rows, cols)

    def cap(self, name):
        t, rows, cols = self.const_tiles[name]
        return t.ap(rows=rows)

    def free_consts(self, names):
        for nm in names:
            if nm in ("dmask_t", "dq", "dk"):
                for h in range(HEADS):
                    self.const_tiles.pop(f"{nm}{h}")[0].free()
            else:
                self.const_tiles.pop(nm)[0].free()

    def ckey(self, name):
        return self.const_tiles[name][0]

    def _norm_keys(self, ks):
        out = []
        for k in ks:
            out.append(k.key if isinstance(k, _Tile) else k)
        return out

    def op(self, eng, fn, reads=(), writes=(), tag=""):
        return self._emit(eng, fn, reads, writes, tag)

    def dma(self, q, fn, reads=(), writes=(), tag=""):
        return self._emit(q, fn, reads, writes, tag)

    def _emit(self, stream, fn, reads, writes, tag):
        is_dma = stream in QUEUES
        emit_eng = Q_ISSUER.get(stream, stream)
        reads = self._norm_keys(reads)
        writes = list(self._norm_keys(writes))
        prev_use = None
        if is_dma:
            slot = self.dma_rr % NPOOL
            self.dma_rr += 1
            self.dma_uses[slot] += 1
            token = (f"d{slot}", 16 * self.dma_uses[slot])
            if self.dma_uses[slot] > 1:
                prev_use = (f"d{slot}", 16 * (self.dma_uses[slot] - 1))
        else:
            self.ecnt[emit_eng] += 1
            token = (emit_eng, self.ecnt[emit_eng])
        haz = []
        for k in list(writes) + list(reads):
            pr = self.pending_hazard.pop(k, None)
            if pr:
                haz.extend(pr)
        deps = {}

        def dep(tok):
            s, v = tok
            if v > deps.get(s, 0):
                deps[s] = v

        if prev_use is not None:
            dep(prev_use)
        for k in reads:
            lw = self.last_w.get(k)
            if lw:
                dep(lw)
        for k in list(writes) + haz:
            lw = self.last_w.get(k)
            if lw:
                dep(lw)
            for s, v in self.readers.get(k, {}).items():
                if v > deps.get(s, 0):
                    deps[s] = v
        o = Op(token, fn, tag, is_dma)
        fr = self.frontier[emit_eng]
        for s, dv in sorted(deps.items()):
            if s == emit_eng and not self.paranoid:
                continue
            if fr.get(s, 0) >= dv:
                continue
            o.waits.append((s, dv))
            fr[s] = dv
            sn = self.snap.get((s, dv))
            if sn:
                for s2, v2 in sn.items():
                    if v2 > fr.get(s2, 0):
                        fr[s2] = v2
        mysnap = dict(fr)
        mysnap[token[0]] = token[1]
        self.snap[token] = mysnap
        for k in writes:
            self.last_w[k] = token
            self.readers[k] = {}
        for k in reads:
            rd = self.readers.setdefault(k, {})
            if token[1] > rd.get(token[0], 0):
                rd[token[0]] = token[1]
        self.ops_by_eng[emit_eng].append(o)
        return token

    def finalize(self, final_eng="sp"):
        nc = self.nc
        sems = {e: self.stack.enter_context(nc.semaphore(f"sem_{e}"))
                for e in ENGINES}
        for i in range(NPOOL):
            if self.dma_uses[i] > 0:
                sems[f"d{i}"] = self.stack.enter_context(
                    nc.semaphore(f"semd{i}"))
        blk = self.stack.enter_context(nc.Block())
        eng_map = {"pe": blk.tensor, "act": blk.scalar, "dve": blk.vector,
                   "pool": blk.gpsimd, "sp": blk.sync}
        eng_obj = {"pe": nc.tensor, "act": nc.scalar, "dve": nc.vector,
                   "pool": nc.gpsimd, "sp": nc.sync}

        def mk_runner(ename):
            ops = self.ops_by_eng[ename]
            eo = eng_obj[ename]

            def body(_):
                for o in ops:
                    for (s, v) in o.waits:
                        eo.wait_ge(sems[s], v)
                    inst = o.fn()
                    inst.then_inc(sems[o.token[0]], 16 if o.is_dma else 1)
                if ename == final_eng:
                    for i in range(NPOOL):
                        if self.dma_uses[i] > 0:
                            eo.wait_ge(sems[f"d{i}"], 16 * self.dma_uses[i])
            return body

        for ename, section in eng_map.items():
            if self.ops_by_eng[ename] or ename == final_eng:
                section(mk_runner(ename))

    def close(self):
        self.stack.close()


def q_quarter(P, q):
    return P.psum["P0"][:, q * FS:(q + 1) * FS], ("P0", q)


def p0_keys():
    return [("P0", q) for q in range(4)]


def emit_weight_stash(P, fp32_name, stash, natbf, tag, fwht=False):
    nc = P.nc
    src = P.dram_in[fp32_name]
    nrow, ncol = src.shape
    rows_per = max(128, (8 << 20) // (ncol * 4) // 128 * 128)
    r = 0
    while r < nrow:
        rr = min(rows_per, nrow - r)
        P.dma("qpool", lambda src=src, natbf=natbf, r=r, rr=rr:
              nc.gpsimd.dma_start(out=natbf.ap()[r:r + rr, :],
                                  in_=src[r:r + rr, :]),
              writes=[(tag, "natbf")], tag=f"{tag}_cast{r}")
        r += rr

    if not fwht:
        nplanes = ncol // 128
        pl = [P.tile_b(nrow, f"{tag}_pl") for _ in range(2)]
        for ic in range(nplanes):
            t = pl[ic % 2]
            P.dma("qsp", lambda t=t, natbf=natbf, ic=ic:
                  nc.sync.dma_start(out=t.ap(),
                                    in_=natbf.ap()[:, ic * 128:(ic + 1) * 128],
                                    transpose=True),
                  reads=[(tag, "natbf")], writes=[t], tag=f"{tag}_tr{ic}")
            P.dma("qact", lambda t=t, stash=stash, ic=ic:
                  nc.scalar.dma_start(out=stash.ap()[ic * 128:(ic + 1) * 128, :],
                                      in_=t.ap()),
                  reads=[t], writes=[(stash.name, "stash")],
                  tag=f"{tag}_dump{ic}")
        for t in pl:
            t.free()
        return

    assert (nrow, ncol) == (D, DFF)
    scale = 1.0 / math.sqrt(float(DFF))
    h128 = P.cap("h128")
    at = [P.tile_b(FS, f"{tag}_A") for _ in range(64)]
    bt = [P.tile_b(FS, f"{tag}_B") for _ in range(64)]
    P1 = P.psum["P1"]
    for s in range(4):
        for hc in range(64):
            t = at[hc]
            par = hc % 2
            P.dma("qsp", lambda t=t, natbf=natbf, s=s, hc=hc:
                  nc.sync.dma_start(out=t.ap(),
                                    in_=natbf.ap()[s * 512:(s + 1) * 512,
                                                   hc * 128:(hc + 1) * 128],
                                    transpose=True),
                  reads=[(tag, "natbf")], writes=[t], tag=f"{tag}_trs")
            P.op("pe", lambda t=t, par=par, h128=h128:
                 nc.tensor.matmul(P1[:, par * 512:(par + 1) * 512], h128,
                                  t.ap(), start=True, stop=True),
                 reads=[t, P.ckey("h128")], writes=[("P1", par)],
                 tag=f"{tag}_h128")
            P.op("dve", lambda t=t, par=par:
                 nc.vector.tensor_scalar_mul(
                     t.ap(), P1[:, par * 512:(par + 1) * 512], scale),
                 reads=[("P1", par)], writes=[t], tag=f"{tag}_evA")
        cur, nxt = at, bt
        h = 1
        while h < 64:
            for blk in range(64 // (2 * h)):
                for k in range(h):
                    a = blk * 2 * h + k
                    b = a + h
                    P.op("dve", lambda cur=cur, nxt=nxt, a=a, b=b:
                         nc.vector.tensor_add(nxt[a].ap(), cur[a].ap(),
                                              cur[b].ap()),
                         reads=[cur[a], cur[b]], writes=[nxt[a]],
                         tag=f"{tag}_bfa")
                    P.op("dve", lambda cur=cur, nxt=nxt, a=a, b=b:
                         nc.vector.tensor_sub(nxt[b].ap(), cur[a].ap(),
                                              cur[b].ap()),
                         reads=[cur[a], cur[b]], writes=[nxt[b]],
                         tag=f"{tag}_bfs")
            cur, nxt = nxt, cur
            h *= 2
        for hc in range(64):
            t = cur[hc]
            P.dma("qact", lambda t=t, stash=stash, hc=hc, s=s:
                  nc.scalar.dma_start(
                      out=stash.ap()[hc * 128:(hc + 1) * 128,
                                     s * 512:(s + 1) * 512],
                      in_=t.ap()),
                  reads=[t], writes=[(stash.name, "stash")],
                  tag=f"{tag}_dump")
    for t in at + bt:
        t.free()


def wslice_ap(stash, ic0, nic, c0, c1):
    return stash.ap().rearrange("(ic p) c -> p ic c", p=128)[:, ic0:ic0 + nic,
                                                            c0:c1]


def xr_ap(xres, ic, c0=0, c1=TPC):
    return xres.ap()[ic * 128:(ic + 1) * 128, c0:c1]


def xr_key(xres, ic):
    return ("xres", xres.name, ic)


def emit_ln(P, xin, tag):
    nc = P.nc
    xt = [P.tile_b(TPC, f"{tag}_x{ic}") for ic in range(FC)]
    sq = [P.tile_b(TPC, f"{tag}_sq") for _ in range(2)]
    ones = P.cap("ones_col")
    P0 = P.psum["P0"]
    for ic in range(FC):
        t = xt[ic]
        P.dma("qact", lambda t=t, xin=xin, ic=ic:
              nc.scalar.dma_start(out=t.ap(), in_=xr_ap(xin, ic)),
              reads=[xr_key(xin, ic)], writes=[t], tag=f"{tag}_ldx{ic}")
        for f in range(4):
            P.op("pe", lambda t=t, f=f, ones=ones, ic=ic:
                 nc.tensor.matmul(P0[0:1, f * FS:(f + 1) * FS], ones,
                                  t.ap(c0=f * FS, c1=(f + 1) * FS),
                                  start=(ic == 0), stop=(ic == FC - 1)),
                 reads=[t, P.ckey("ones_col")], writes=[("P0", f)],
                 tag=f"{tag}_msum")
    rows = P.tile_f(3 * TPC, f"{tag}_rows")

    def rowap(i):
        return rows.arena[0:1, rows.off + i * TPC:rows.off + (i + 1) * TPC]

    rk = [(f"{tag}_row", i) for i in range(3)]
    P.op("dve", lambda: nc.vector.tensor_scalar_mul(rowap(0), P0[0:1, :],
                                                    1.0 / D),
         reads=p0_keys(), writes=[rows, rk[0]], tag=f"{tag}_mean")
    for ic in range(FC):
        t = xt[ic]
        q = sq[ic % 2]
        P.op("act", lambda q=q, t=t: nc.scalar.activation(q.ap(), t.ap(),
                                                          AF.Square),
             reads=[t], writes=[q], tag=f"{tag}_sq")
        for f in range(4):
            P.op("pe", lambda q=q, f=f, ones=ones, ic=ic:
                 nc.tensor.matmul(P0[0:1, f * FS:(f + 1) * FS], ones,
                                  q.ap(c0=f * FS, c1=(f + 1) * FS),
                                  start=(ic == 0), stop=(ic == FC - 1)),
                 reads=[q, P.ckey("ones_col")], writes=[("P0", f)],
                 tag=f"{tag}_vsum")
    P.op("dve", lambda: nc.vector.tensor_scalar_mul(rowap(1), P0[0:1, :],
                                                    1.0 / D),
         reads=p0_keys() + [rows.key], writes=[rk[1]], tag=f"{tag}_ex2")
    P.op("dve", lambda: nc.vector.tensor_mul(rowap(2), rowap(0), rowap(0)),
         reads=[rk[0]], writes=[rk[2]], tag=f"{tag}_m2")
    P.op("dve", lambda: nc.vector.tensor_sub(rowap(1), rowap(1), rowap(2)),
         reads=[rk[1], rk[2]], writes=[rk[1]], tag=f"{tag}_var")
    P.op("act", lambda: nc.scalar.activation(rowap(2), rowap(1), AF.Ln,
                                             bias=np.float32(EPS_LN)),
         reads=[rk[1]], writes=[rk[2]], tag=f"{tag}_lnv")
    P.op("act", lambda: nc.scalar.activation(rowap(2), rowap(2), AF.Exp,
                                             scale=np.float32(-0.5)),
         reads=[rk[2]], writes=[rk[2]], tag=f"{tag}_exp")
    P.op("dve", lambda: nc.vector.scalar_tensor_tensor(
            rowap(1), rowap(0), -1.0, rowap(2), ALU.mult, ALU.mult),
         reads=[rk[0], rk[2]], writes=[rk[1]], tag=f"{tag}_brow")
    rb = P.tile_b(2 * TPC, f"{tag}_rb")
    rbk = [(f"{tag}_rb", i) for i in range(2)]

    def rbap(i):
        return rb.arena[0:1, rb.off + i * TPC:rb.off + (i + 1) * TPC]

    P.op("dve", lambda: nc.vector.tensor_copy(rbap(0), rowap(2)),
         reads=[rk[2]], writes=[rb, rbk[0]], tag=f"{tag}_rba")
    P.op("dve", lambda: nc.vector.tensor_copy(rbap(1), rowap(1)),
         reads=[rk[1]], writes=[rbk[1]], tag=f"{tag}_rbb")
    onesr = P.cap("ones_row")
    Abc = P.tile_b(TPC, f"{tag}_Abc")
    Bbc = P.tile_b(TPC, f"{tag}_Bbc")
    for ri, dst in ((0, Abc), (1, Bbc)):
        for f in range(4):
            P.op("pe", lambda ri=ri, f=f, onesr=onesr:
                 nc.tensor.matmul(P0[:, f * FS:(f + 1) * FS], onesr,
                                  rbap(ri)[0:1, f * FS:(f + 1) * FS],
                                  start=True, stop=True),
                 reads=[rbk[ri], P.ckey("ones_row")],
                 writes=[("P0", f)], tag=f"{tag}_bc")
        P.op("dve", lambda dst=dst: nc.vector.tensor_copy(dst.ap(), P0[:, :]),
             reads=p0_keys(), writes=[dst], tag=f"{tag}_evbc")
    for ic in range(FC):
        t = xt[ic]
        P.op("dve", lambda t=t, Abc=Abc: nc.vector.tensor_mul(
                t.ap(), t.ap(), Abc.ap()),
             reads=[t, Abc], writes=[t], tag=f"{tag}_mA")
        P.op("dve", lambda t=t, Bbc=Bbc: nc.vector.tensor_add(
                t.ap(), t.ap(), Bbc.ap()),
             reads=[t, Bbc], writes=[t], tag=f"{tag}_aB")
    lnt = xt
    sq[0].free()
    sq[1].free()
    rows.free()
    rb.free()
    Abc.free()
    Bbc.free()
    return lnt


def emit_ffn(P, tag, xin, xout, w1stash, w3stash, w2stash, hid, scale=0.5):
    nc = P.nc
    lnt = emit_ln(P, xin, tag + "l")
    P0 = P.psum["P0"]
    P1 = P.psum["P1"]

    w1sl = [P.tile_b(FC * FS, f"{tag}_w1s") for _ in range(2)]
    w3sl = [P.tile_b(FC * FS, f"{tag}_w3s") for _ in range(2)]
    tg = [P.tile_b(FS, f"{tag}_tg") for _ in range(2)]
    hstage = [P.tile_b(FS, f"{tag}_hst") for _ in range(2)]
    NHG = DFF // FS
    for hg in range(NHG):
        bsl = hg % 2
        for (wst, wt) in ((w1stash, w1sl[bsl]), (w3stash, w3sl[bsl])):
            P.dma("qsp", lambda wt=wt, wst=wst, hg=hg:
                  nc.sync.dma_start(out=wt.ap3((FC, FS)),
                                    in_=wslice_ap(wst, 0, FC, hg * FS,
                                                  (hg + 1) * FS)),
                  reads=[(wst.name, "stash")], writes=[wt],
                  tag=f"{tag}_ldw{hg}")
        for hcl in range(FS // 128):
            hc = hg * 4 + hcl
            for f in range(4):
                par = (hc * 4 + f) % 2
                gq, uq = 2 * par, 2 * par + 1
                gap, gkey = q_quarter(P, gq)
                uap, ukey = q_quarter(P, uq)
                for ic in range(FC):
                    co = ic * FS + hcl * 128
                    P.op("pe", lambda wt=w1sl[bsl], co=co, f=f, gap=gap, ic=ic:
                         nc.tensor.matmul(gap, wt.ap(c0=co, c1=co + 128),
                                          lnt[ic].ap(c0=f * FS, c1=(f + 1) * FS),
                                          start=(ic == 0), stop=(ic == FC - 1)),
                         reads=[w1sl[bsl], lnt[ic]], writes=[gkey],
                         tag=f"{tag}_ug")
                    P.op("pe", lambda wt=w3sl[bsl], co=co, f=f, uap=uap, ic=ic:
                         nc.tensor.matmul(uap, wt.ap(c0=co, c1=co + 128),
                                          lnt[ic].ap(c0=f * FS, c1=(f + 1) * FS),
                                          start=(ic == 0), stop=(ic == FC - 1)),
                         reads=[w3sl[bsl], lnt[ic]], writes=[ukey],
                         tag=f"{tag}_uu")
                tgt, hst = tg[par], hstage[par]
                P.op("act", lambda tgt=tgt, gap=gap:
                     nc.scalar.activation(tgt.ap(), gap, AF.Silu),
                     reads=[gkey], writes=[tgt], tag=f"{tag}_silu")
                P.op("dve", lambda hst=hst, tgt=tgt, uap=uap:
                     nc.vector.tensor_mul(hst.ap(), tgt.ap(), uap),
                     reads=[tgt, ukey], writes=[hst], tag=f"{tag}_hm")
                P.dma("qact", lambda hst=hst, hc=hc, f=f:
                      nc.scalar.dma_start(
                          out=hid.ap()[hc * 128:(hc + 1) * 128,
                                       f * FS:(f + 1) * FS],
                          in_=hst.ap()),
                      reads=[hst], writes=[("hid", hid.name, hc)],
                      tag=f"{tag}_hsp")
    for t in w1sl + w3sl + tg + hstage:
        t.free()
    for t in lnt:
        t.free()

    hts = [P.tile_b(FS, f"{tag}_ht") for _ in range(HC)]
    w2sl = [P.tile_b(32 * 256, f"{tag}_w2s") for _ in range(4)]
    xres = [P.tile_b(FS, f"{tag}_xr") for _ in range(2)]
    ostg = [P.tile_b(FS, f"{tag}_os") for _ in range(2)]
    for ts in range(4):
        fo = ts * FS
        for hc in range(HC):
            P.dma("qsp", lambda hc=hc, fo=fo, t=hts[hc]:
                  nc.sync.dma_start(out=t.ap(),
                                    in_=hid.ap()[hc * 128:(hc + 1) * 128,
                                                 fo:fo + FS]),
                  reads=[("hid", hid.name, hc)], writes=[hts[hc]],
                  tag=f"{tag}_hld")

        def emit_wload(i, ts=ts):
            ocg, ich = divmod(i, 2)
            wt = w2sl[i % 4]
            P.dma("qsp", lambda wt=wt, ocg=ocg, ich=ich:
                  nc.sync.dma_start(out=wt.ap3((32, 256)),
                                    in_=wslice_ap(w2stash, ich * 32, 32,
                                                  ocg * 256, (ocg + 1) * 256)),
                  reads=[(w2stash.name, "stash")], writes=[wt],
                  tag=f"{tag}_ldw2")

        for i in range(4):
            emit_wload(i)
        li = 4
        for ocg in range(8):
            for oc in range(2):
                occ = ocg * 2 + oc
                par = occ % 2
                dap = P1[:, par * FS:(par + 1) * FS]
                dkey = ("P1", par)
                for ich in range(2):
                    wt = w2sl[(ocg * 2 + ich) % 4]
                    for icl in range(32):
                        ic = ich * 32 + icl
                        co = icl * 256 + oc * 128
                        P.op("pe", lambda wt=wt, co=co, ic=ic, dap=dap:
                             nc.tensor.matmul(
                                 dap, wt.ap(c0=co, c1=co + 128),
                                 hts[ic].ap(),
                                 start=(ic == 0), stop=(ic == HC - 1)),
                             reads=[wt, hts[ic]], writes=[dkey],
                             tag=f"{tag}_dn")
                xr, os_ = xres[par], ostg[par]
                P.dma("qact", lambda xr=xr, occ=occ, fo=fo:
                      nc.scalar.dma_start(
                          out=xr.ap(), in_=xr_ap(xin, occ, fo, fo + FS)),
                      reads=[xr_key(xin, occ)], writes=[xr],
                      tag=f"{tag}_lre")
                P.op("dve", lambda os_=os_, xr=xr, dap=dap:
                     nc.vector.scalar_tensor_tensor(
                         os_.ap(), dap, scale, xr.ap(), ALU.mult, ALU.add),
                     reads=[dkey, xr], writes=[os_], tag=f"{tag}_res")
                P.dma("qact", lambda os_=os_, occ=occ, fo=fo:
                      nc.scalar.dma_start(
                          out=xout.ap()[occ * 128:(occ + 1) * 128, fo:fo + FS],
                          in_=os_.ap()),
                      reads=[os_], writes=[xr_key(xout, occ)],
                      tag=f"{tag}_stx")
            if li < 16:
                emit_wload(li)
                emit_wload(li + 1)
                li += 2
    for t in hts + w2sl + xres + ostg:
        t.free()


def emit_proj(P, tag, src_tiles, wname, noc, evac, stash=None, oc_list=None):
    nc = P.nc
    nic = len(src_tiles)
    wsl = [P.tile_b(nic * 128, f"{tag}_w") for _ in range(2)]

    def emit_wload(oc):
        wt = wsl[oc % 2]
        if stash is None:
            src = P.dram_in[wname].rearrange("(ic p) o -> p ic o", p=128)
            P.dma("qpool", lambda wt=wt, oc=oc:
                  nc.gpsimd.dma_start(out=wt.ap3((nic, 128)),
                                      in_=src[:, :, oc * 128:(oc + 1) * 128]),
                  writes=[wt], tag=f"{tag}_w{oc}")
        else:
            P.dma("qsp", lambda wt=wt, oc=oc:
                  nc.sync.dma_start(out=wt.ap3((nic, 128)),
                                    in_=wslice_ap(stash, 0, nic, oc * 128,
                                                  (oc + 1) * 128)),
                  reads=[(stash.name, "stash")], writes=[wt],
                  tag=f"{tag}_w{oc}")

    oc_list = list(range(noc)) if oc_list is None else oc_list
    emit_wload(oc_list[0])
    for n_, oc in enumerate(oc_list):
        if n_ + 1 < len(oc_list):
            emit_wload(oc_list[n_ + 1])
        wt = wsl[oc % 2]
        for f in range(4):
            par = (oc * 4 + f) % 2
            pq = 2 * par
            pap, pkey = q_quarter(P, pq)
            for ic in range(nic):
                P.op("pe", lambda wt=wt, ic=ic, f=f, pap=pap:
                     nc.tensor.matmul(pap, wt.ap(c0=ic * 128, c1=(ic + 1) * 128),
                                      src_tiles[ic].ap(c0=f * FS,
                                                       c1=(f + 1) * FS),
                                      start=(ic == 0), stop=(ic == nic - 1)),
                     reads=[wt, src_tiles[ic]], writes=[pkey],
                     tag=f"{tag}_mm")
            evac(oc, f, pap, pkey, par)
    wsl[0].free()
    wsl[1].free()


def emit_rotary(P, tag, xt, cc, ss, sw_scr):
    nc = P.nc
    P2 = P.psum["P2"]
    swap = P.cap("swap")
    for f in range(4):
        P.op("pe", lambda f=f, xt=xt, swap=swap:
             nc.tensor.matmul(P2[:, 0:FS], swap,
                              xt.ap(c0=f * FS, c1=(f + 1) * FS),
                              start=True, stop=True),
             reads=[xt, P.ckey("swap")], writes=[("P2",)], tag=f"{tag}_sw")
        P.op("dve", lambda f=f, sw_scr=sw_scr, ss=ss:
             nc.vector.tensor_mul(sw_scr.ap(), P2[:, 0:FS],
                                  ss.ap(c0=f * FS, c1=(f + 1) * FS)),
             reads=[("P2",), ss], writes=[sw_scr], tag=f"{tag}_ssm")
        P.op("dve", lambda f=f, xt=xt, cc=cc:
             nc.vector.tensor_mul(xt.ap(c0=f * FS, c1=(f + 1) * FS),
                                  xt.ap(c0=f * FS, c1=(f + 1) * FS),
                                  cc.ap(c0=f * FS, c1=(f + 1) * FS)),
             reads=[xt, cc], writes=[xt], tag=f"{tag}_ccm")
        P.op("dve", lambda f=f, xt=xt, sw_scr=sw_scr:
             nc.vector.tensor_add(xt.ap(c0=f * FS, c1=(f + 1) * FS),
                                  xt.ap(c0=f * FS, c1=(f + 1) * FS),
                                  sw_scr.ap()),
             reads=[xt, sw_scr], writes=[xt], tag=f"{tag}_add")


def emit_attn0_phaseA(P, x1, qsp, ksp, vtsp, kvsp, s_out, gtmp):
    nc = P.nc
    tag = "a0A"
    lnt = emit_ln(P, x1, tag + "l")
    cc_t, ss_t = P.const_tiles["rope_cc"][0], P.const_tiles["rope_ss"][0]
    ident = P.cap("ident")
    PB = P.psum["PB"]
    P2 = P.psum["P2"]

    gstage = P.tile_b(TPC, f"{tag}_gst")

    def gevac(oc, f, pap, pkey, par):
        P.op("dve", lambda f=f, pap=pap:
             nc.vector.tensor_copy(gstage.ap(c0=f * FS, c1=(f + 1) * FS), pap),
             reads=[pkey], writes=[gstage], tag=f"{tag}_gev")
        if f == 3:
            P.dma("qact", lambda: nc.scalar.dma_start(out=gtmp.ap(),
                                                      in_=gstage.ap()),
                  reads=[gstage], writes=[("gtmp",)], tag=f"{tag}_gsp")

    emit_proj(P, tag + "g", lnt, "L0_attn_Wg1", 1, gevac)
    gstage.free()

    qt = P.tile_b(TPC, f"{tag}_qt")
    kt = P.tile_b(TPC, f"{tag}_kt")
    vt = P.tile_b(TPC, f"{tag}_vt")
    sw_scr = P.tile_b(FS, f"{tag}_sws")
    kd = P.tile_b(CK, f"{tag}_kd")
    kdT = P.tile_b(CK, f"{tag}_kdT")
    vTt = P.tile_b(CK, f"{tag}_vT")
    sst = [P.tile_f(CK, f"{tag}_S") for _ in range(2)]

    def mk_evac(dst, act):
        def evac(oc, f, pap, pkey, par):
            if act:
                P.op("act", lambda f=f, pap=pap, dst=dst:
                     nc.scalar.activation(dst.ap(c0=f * FS, c1=(f + 1) * FS),
                                          pap, AF.Silu),
                     reads=[pkey], writes=[dst], tag=f"{tag}_silu")
            else:
                P.op("dve", lambda f=f, pap=pap, dst=dst:
                     nc.vector.tensor_copy(dst.ap(c0=f * FS, c1=(f + 1) * FS),
                                           pap),
                     reads=[pkey], writes=[dst], tag=f"{tag}_vev")
        return evac

    for h in range(HEADS):
        emit_proj(P, f"{tag}q{h}", lnt, "L0_attn_Wq", HEADS,
                  mk_evac(qt, True), oc_list=[h])
        emit_proj(P, f"{tag}k{h}", lnt, "L0_attn_Wk", HEADS,
                  mk_evac(kt, True), oc_list=[h])
        emit_proj(P, f"{tag}v{h}", lnt, "L0_attn_Wv", HEADS,
                  mk_evac(vt, False), oc_list=[h])
        emit_rotary(P, f"{tag}rq{h}", qt, cc_t, ss_t, sw_scr)
        emit_rotary(P, f"{tag}rk{h}", kt, cc_t, ss_t, sw_scr)
        P.dma("qact", lambda h=h: nc.scalar.dma_start(
                  out=qsp.ap()[h * 128:(h + 1) * 128, :], in_=qt.ap()),
              reads=[qt], writes=[("qsp", h)], tag=f"{tag}_qsp")
        P.dma("qact", lambda h=h: nc.scalar.dma_start(
                  out=ksp.ap()[h * 128:(h + 1) * 128, :], in_=kt.ap()),
              reads=[kt], writes=[("ksp", h)], tag=f"{tag}_ksp")
        lam_c = float(math.exp(LOG_LAM[h] * CK))
        dkc = P.const_tiles[f"dk{h}"][0]
        for b_ in range(2):
            S = sst[b_]
            P.op("dve", lambda S=S: nc.vector.memset(S.ap(), 0.0),
                 reads=[], writes=[S], tag=f"{tag}_S0")
            for j in range(JPC):
                c0 = b_ * NPC + j * CK
                P.op("dve", lambda c0=c0, dkc=dkc:
                     nc.vector.tensor_mul(kd.ap(), kt.ap(c0=c0, c1=c0 + CK),
                                          dkc.ap()),
                     reads=[kt, dkc], writes=[kd], tag=f"{tag}_kd")
                P.op("pe", lambda: nc.tensor.transpose(PB[:, 0:CK], kd.ap(),
                                                       ident),
                     reads=[kd, P.ckey("ident")], writes=[("PB",)],
                     tag=f"{tag}_tkd")
                P.op("dve", lambda: nc.vector.tensor_copy(kdT.ap(),
                                                          PB[:, 0:CK]),
                     reads=[("PB",)], writes=[kdT], tag=f"{tag}_ekd")
                P.op("pe", lambda c0=c0: nc.tensor.transpose(
                         PB[:, 0:CK], vt.ap(c0=c0, c1=c0 + CK), ident),
                     reads=[vt, P.ckey("ident")], writes=[("PB",)],
                     tag=f"{tag}_tv")
                P.op("dve", lambda: nc.vector.tensor_copy(vTt.ap(),
                                                          PB[:, 0:CK]),
                     reads=[("PB",)], writes=[vTt], tag=f"{tag}_ev")
                row = (h * 2 * JPC + b_ * JPC + j) * 128
                P.dma("qact", lambda row=row: nc.scalar.dma_start(
                          out=vtsp.ap()[row:row + 128, :], in_=vTt.ap()),
                      reads=[vTt], writes=[("vtsp", row)], tag=f"{tag}_vsp")
                P.op("pe", lambda: nc.tensor.matmul(P2[:, 0:CK], kdT.ap(),
                                                    vTt.ap(), start=True,
                                                    stop=True),
                     reads=[kdT, vTt], writes=[("P2",)], tag=f"{tag}_kv")
                P.op("dve", lambda S=S, lam_c=lam_c:
                     nc.vector.scalar_tensor_tensor(
                         S.ap(), S.ap(), lam_c, P2[:, 0:CK],
                         ALU.mult, ALU.add),
                     reads=[S, ("P2",)], writes=[S], tag=f"{tag}_Sup")
                kvst = P.tile_f(CK, f"{tag}_kvs")
                P.op("dve", lambda kvst=kvst: nc.vector.tensor_copy(
                         kvst.ap(), P2[:, 0:CK]),
                     reads=[("P2",)], writes=[kvst], tag=f"{tag}_ekv")
                P.dma("qact", lambda kvst=kvst, row=row: nc.scalar.dma_start(
                          out=kvsp.ap()[row:row + 128, :], in_=kvst.ap()),
                      reads=[kvst], writes=[("kvsp", row)], tag=f"{tag}_kvsp")
                kvst.free()
            srow = (b_ * HEADS + h) * 128
            P.dma("qact", lambda S=S, srow=srow: nc.scalar.dma_start(
                      out=s_out.ap()[srow:srow + 128, :], in_=S.ap()),
                  reads=[S], writes=[("s_out", srow)], tag=f"{tag}_ssp")
    for t in [qt, kt, vt, sw_scr, kd, kdT, vTt] + sst + lnt:
        t.free()


def emit_attn0_phaseB(P, x1, x2, qsp, ksp, vtsp, kvsp, s_all, gtmp):
    nc = P.nc
    tag = "a0B"
    P0 = P.psum["P0"]
    P1 = P.psum["P1"]
    sprefw = P.const_tiles["sprefw"][0]

    st = [[P.tile_f(CK, f"{tag}_S{b_}_{h}") for h in range(HEADS)]
          for b_ in range(2)]
    sld = [P.tile_f(CK, f"{tag}_sld") for _ in range(2)]
    for b_ in range(2):
        for h in range(HEADS):
            S = st[b_][h]
            P.op("dve", lambda S=S: nc.vector.memset(S.ap(), 0.0),
                 reads=[], writes=[S], tag=f"{tag}_S0")
            for m in range(NCORE):
                t = sld[m % 2]
                row = (m * 2 * HEADS + b_ * HEADS + h) * 128
                P.dma("qsp", lambda t=t, row=row: nc.sync.dma_start(
                          out=t.ap(), in_=s_all.ap()[row:row + 128, :]),
                      reads=[], writes=[t], tag=f"{tag}_sldm")
                col = m * HEADS + h
                P.op("dve", lambda S=S, t=t, col=col:
                     nc.vector.scalar_tensor_tensor(
                         S.ap(), t.ap(), sprefw.ap(c0=col, c1=col + 1),
                         S.ap(), ALU.mult, ALU.add),
                     reads=[S, t, P.const_tiles["sprefw"][0]], writes=[S],
                     tag=f"{tag}_sacc")
    sld[0].free()
    sld[1].free()

    gw = P.tile_b(D, f"{tag}_gw")
    P.dma("qpool", lambda: nc.gpsimd.dma_start(out=gw.ap(),
                                               in_=P.dram_in["L0_attn_Wg2"]),
          writes=[gw], tag=f"{tag}_gw")
    gt = P.tile_b(TPC, f"{tag}_gt")
    P.dma("qsp", lambda: nc.sync.dma_start(out=gt.ap(), in_=gtmp.ap()),
          reads=[("gtmp",)], writes=[gt], tag=f"{tag}_gt")
    nwn = P.nc.dram_tensor("nw_bf", [FC, 128], BF16, kind="Internal")
    P.dma("qpool", lambda: nc.gpsimd.dma_start(
              out=nwn.ap(), in_=P.dram_in["L0_attn_norm_weight"].rearrange(
                  "(a b) -> a b", b=128)),
          writes=[("nwn",)], tag=f"{tag}_nwc")
    nwb = P.tile_b(FC, f"{tag}_nwb")
    P.dma("qsp", lambda: nc.sync.dma_start(out=nwb.ap(), in_=nwn.ap(),
                                           transpose=True),
          reads=[("nwn",)], writes=[nwb], tag=f"{tag}_nwt")
    nw = P.tile_f(FC, f"{tag}_nw")
    P.op("dve", lambda: nc.vector.tensor_copy(nw.ap(), nwb.ap()),
         reads=[nwb], writes=[nw], tag=f"{tag}_nwf")

    ot = [P.tile_b(TPC, f"{tag}_o{h}") for h in range(HEADS)]
    qh = P.tile_b(TPC, f"{tag}_qh")
    kh = P.tile_b(TPC, f"{tag}_kh")
    vtl = [P.tile_b(CK, f"{tag}_vtl") for _ in range(2)]
    kvl = [P.tile_f(CK, f"{tag}_kvl") for _ in range(2)]
    scm = P.tile_b(CK, f"{tag}_scm")
    qd = P.tile_b(CK, f"{tag}_qd")
    sbf = P.tile_b(CK, f"{tag}_sbf")
    sq = P.tile_b(TPC, f"{tag}_sq")
    sg = P.tile_b(TPC, f"{tag}_sg")
    rows = P.tile_f(2 * TPC, f"{tag}_rows")
    rbr = P.tile_b(TPC, f"{tag}_rbr")

    for h in range(HEADS):
        P.dma("qsp", lambda h=h: nc.sync.dma_start(
                  out=qh.ap(), in_=qsp.ap()[h * 128:(h + 1) * 128, :]),
              reads=[], writes=[qh], tag=f"{tag}_ldq")
        P.dma("qsp", lambda h=h: nc.sync.dma_start(
                  out=kh.ap(), in_=ksp.ap()[h * 128:(h + 1) * 128, :]),
              reads=[], writes=[kh], tag=f"{tag}_ldk")
        dmc = P.const_tiles[f"dmask_t{h}"][0]
        dqc = P.const_tiles[f"dq{h}"][0]
        lam_c = float(math.exp(LOG_LAM[h] * CK))
        oh = ot[h]
        for b_ in range(2):
            S = st[b_][h]
            for j in range(JPC):
                c0 = b_ * NPC + j * CK
                row = (h * 2 * JPC + b_ * JPC + j) * 128
                vl = vtl[j % 2]
                kl = kvl[j % 2]
                P.dma("qsp", lambda vl=vl, row=row: nc.sync.dma_start(
                          out=vl.ap(), in_=vtsp.ap()[row:row + 128, :]),
                      reads=[], writes=[vl], tag=f"{tag}_lv")
                P.dma("qsp", lambda kl=kl, row=row: nc.sync.dma_start(
                          out=kl.ap(), in_=kvsp.ap()[row:row + 128, :]),
                      reads=[], writes=[kl], tag=f"{tag}_lkv")
                P.op("pe", lambda c0=c0: nc.tensor.matmul(
                         P1[:, 0:CK], kh.ap(c0=c0, c1=c0 + CK),
                         qh.ap(c0=c0, c1=c0 + CK), start=True, stop=True),
                     reads=[kh, qh], writes=[("P1", 0)], tag=f"{tag}_sc")
                P.op("dve", lambda dmc=dmc: nc.vector.tensor_mul(
                         scm.ap(), P1[:, 0:CK], dmc.ap()),
                     reads=[("P1", 0), dmc], writes=[scm], tag=f"{tag}_msk")
                P.op("pe", lambda vl=vl: nc.tensor.matmul(
                         P1[:, 512:512 + CK], vl.ap(), scm.ap(),
                         start=True, stop=False),
                     reads=[vl, scm], writes=[("P1", 1)], tag=f"{tag}_oi")
                P.op("dve", lambda c0=c0, dqc=dqc: nc.vector.tensor_mul(
                         qd.ap(), qh.ap(c0=c0, c1=c0 + CK), dqc.ap()),
                     reads=[qh, dqc], writes=[qd], tag=f"{tag}_qd")
                P.op("dve", lambda S=S: nc.vector.tensor_copy(sbf.ap(),
                                                              S.ap()),
                     reads=[S], writes=[sbf], tag=f"{tag}_sbf")
                P.op("pe", lambda: nc.tensor.matmul(
                         P1[:, 512:512 + CK], sbf.ap(), qd.ap(),
                         start=False, stop=True),
                     reads=[sbf, qd], writes=[("P1", 1)], tag=f"{tag}_oe")
                P.op("dve", lambda oh=oh, c0=c0: nc.vector.tensor_copy(
                         oh.ap(c0=c0, c1=c0 + CK), P1[:, 512:512 + CK]),
                     reads=[("P1", 1)], writes=[oh], tag=f"{tag}_oev")
                P.op("dve", lambda S=S, kl=kl, lam_c=lam_c:
                     nc.vector.scalar_tensor_tensor(
                         S.ap(), S.ap(), lam_c, kl.ap(), ALU.mult, ALU.add),
                     reads=[S, kl], writes=[S], tag=f"{tag}_Sup")
        if DEBUG_ATTN0:
            P.dma("qact", lambda oh=oh, h=h: nc.scalar.dma_start(
                      out=P.dram_out["od1"].ap()[h * 128:(h + 1) * 128, :],
                      in_=oh.ap()),
                  reads=[oh], writes=[("od1", h)], tag=f"{tag}_dbg1")
        for f in range(4):
            P.op("pe", lambda f=f, h=h: nc.tensor.matmul(
                     P0[:, f * FS:(f + 1) * FS],
                     gw.ap(c0=h * 128, c1=(h + 1) * 128),
                     gt.ap(c0=f * FS, c1=(f + 1) * FS),
                     start=True, stop=True),
                 reads=[gw, gt], writes=[("P0", f)], tag=f"{tag}_gmm")
        P.op("act", lambda: nc.scalar.activation(sg.ap(), P0[:, :],
                                                 AF.Sigmoid),
             reads=p0_keys(), writes=[sg], tag=f"{tag}_sig")
        P.op("dve", lambda oh=oh: nc.vector.tensor_mul(oh.ap(), oh.ap(),
                                                       sg.ap()),
             reads=[oh, sg], writes=[oh], tag=f"{tag}_og")
        P.op("act", lambda oh=oh: nc.scalar.activation(sq.ap(), oh.ap(),
                                                       AF.Square),
             reads=[oh], writes=[sq], tag=f"{tag}_sq2")
        ones = P.cap("ones_col")
        for f in range(4):
            P.op("pe", lambda f=f, ones=ones: nc.tensor.matmul(
                     P0[0:1, f * FS:(f + 1) * FS], ones,
                     sq.ap(c0=f * FS, c1=(f + 1) * FS), start=True, stop=True),
                 reads=[sq, P.ckey("ones_col")], writes=[("P0", f)],
                 tag=f"{tag}_mssum")

        def rowap(i):
            return rows.arena[0:1, rows.off + i * TPC:rows.off + (i + 1) * TPC]

        rk = [(f"{tag}_row{h}", i) for i in range(2)]
        P.op("dve", lambda: nc.vector.tensor_scalar_mul(rowap(0), P0[0:1, :],
                                                        1.0 / HDIM),
             reads=p0_keys(), writes=[rows, rk[0]], tag=f"{tag}_ms")
        P.op("act", lambda: nc.scalar.activation(rowap(1), rowap(0), AF.Ln,
                                                 bias=np.float32(EPS_RMS)),
             reads=[rk[0]], writes=[rk[1]], tag=f"{tag}_lnr")
        P.op("act", lambda: nc.scalar.activation(rowap(1), rowap(1), AF.Exp,
                                                 scale=np.float32(-0.5)),
             reads=[rk[1]], writes=[rk[1]], tag=f"{tag}_expr")
        P.op("dve", lambda: nc.vector.tensor_copy(rbr.ap(rows=1), rowap(1)),
             reads=[rk[1]], writes=[rbr], tag=f"{tag}_rbr")
        onesr = P.cap("ones_row")
        for f in range(4):
            P.op("pe", lambda f=f, onesr=onesr: nc.tensor.matmul(
                     P0[:, f * FS:(f + 1) * FS], onesr,
                     rbr.ap(rows=1, c0=f * FS, c1=(f + 1) * FS),
                     start=True, stop=True),
                 reads=[rbr, P.ckey("ones_row")], writes=[("P0", f)],
                 tag=f"{tag}_bcr")
        P.op("dve", lambda oh=oh: nc.vector.tensor_mul(oh.ap(), oh.ap(),
                                                       P0[:, :]),
             reads=[oh] + p0_keys(), writes=[oh], tag=f"{tag}_orm")
        P.op("dve", lambda oh=oh, h=h: nc.vector.tensor_scalar_mul(
                 oh.ap(), oh.ap(), nw.ap(c0=h, c1=h + 1)),
             reads=[oh, nw], writes=[oh], tag=f"{tag}_onw")
        if DEBUG_ATTN0:
            P.dma("qact", lambda oh=oh, h=h: nc.scalar.dma_start(
                      out=P.dram_out["od2"].ap()[h * 128:(h + 1) * 128, :],
                      in_=oh.ap()),
                  reads=[oh], writes=[("od2", h)], tag=f"{tag}_dbg2")
    qh.free()
    kh.free()
    for t in vtl + kvl + [scm, qd, sbf, sq, sg, rows, rbr, gw, gt, nw, nwb]:
        t.free()
    for b_ in range(2):
        for h in range(HEADS):
            st[b_][h].free()

    xres = [P.tile_b(FS, f"{tag}_xr") for _ in range(2)]
    ostg = [P.tile_b(FS, f"{tag}_os") for _ in range(2)]

    def wo_evac(oc, f, pap, pkey, par):
        xr, os_ = xres[par], ostg[par]
        fo = f * FS
        P.dma("qact", lambda xr=xr, oc=oc, fo=fo: nc.scalar.dma_start(
                  out=xr.ap(), in_=xr_ap(x1, oc, fo, fo + FS)),
              reads=[xr_key(x1, oc)], writes=[xr], tag=f"{tag}_lre")
        P.op("dve", lambda os_=os_, xr=xr, pap=pap:
             nc.vector.scalar_tensor_tensor(os_.ap(), pap, 1.0, xr.ap(),
                                            ALU.mult, ALU.add),
             reads=[pkey, xr], writes=[os_], tag=f"{tag}_res")
        P.dma("qact", lambda os_=os_, oc=oc, fo=fo: nc.scalar.dma_start(
                  out=x2.ap()[oc * 128:(oc + 1) * 128, fo:fo + FS],
                  in_=os_.ap()),
              reads=[os_], writes=[xr_key(x2, oc)], tag=f"{tag}_stx")

    emit_proj(P, tag + "wo", ot, "L0_attn_Wo", FC, wo_evac)
    for t in ot + xres + ostg:
        t.free()


def bias_tile(P, tag, name, nchunks, scratch_name):
    nc = P.nc
    bn = P.nc.dram_tensor(scratch_name, [nchunks, 128], BF16, kind="Internal")
    P.dma("qpool", lambda: nc.gpsimd.dma_start(
              out=bn.ap(), in_=P.dram_in[name].rearrange("(a b) -> a b",
                                                         b=128)),
          writes=[(scratch_name,)], tag=f"{tag}_c")
    bb = P.tile_b(nchunks, f"{tag}_bb")
    P.dma("qsp", lambda: nc.sync.dma_start(out=bb.ap(), in_=bn.ap(),
                                           transpose=True),
          reads=[(scratch_name,)], writes=[bb], tag=f"{tag}_t")
    bf = P.tile_f(nchunks, f"{tag}_bf")
    P.op("dve", lambda: nc.vector.tensor_copy(bf.ap(), bb.ap()),
         reads=[bb], writes=[bf], tag=f"{tag}_f")
    bb.free()
    return bf


def emit_attn1(P, xin, xout, wqkvt, wdenset):
    nc = P.nc
    tag = "a1"
    lnt = emit_ln(P, xin, tag + "l")
    bqkv = bias_tile(P, tag + "bq", "L1_attn_b_qkv", 48, "bqkv_bf")
    bden = bias_tile(P, tag + "bd", "L1_attn_b_dense", FC, "bden_bf")
    P0 = P.psum["P0"]
    ctx = [P.tile_b(TPC, f"{tag}_ctx{h}") for h in range(HEADS)]
    qk = [P.tile_b(TPC, f"{tag}_qk") for _ in range(2)]
    m0 = P.tile_b(1024, f"{tag}_m0")
    rows = P.tile_f(4 * 1024, f"{tag}_rows")
    prow = P.tile_b(TPC, f"{tag}_prow")
    onesc = P.cap("ones_col")
    onesr = P.cap("ones_row")
    SC = 1.0 / math.sqrt(float(HDIM))

    def rowap(i):
        return rows.arena[0:1, rows.off + i * 1024:rows.off + (i + 1) * 1024]

    for h in range(HEADS):
        for which, dst in ((0, qk[0]), (1, qk[1]), (2, ctx[h])):
            oc = 3 * h + which

            def ev(oc_, f, pap, pkey, par, dst=dst, which=which):
                co = 3 * h + which
                P.op("dve", lambda f=f, pap=pap, dst=dst, co=co:
                     nc.vector.tensor_scalar(
                         dst.ap(c0=f * FS, c1=(f + 1) * FS), pap,
                         bqkv.ap(c0=co, c1=co + 1), None, ALU.add),
                     reads=[pkey, bqkv], writes=[dst], tag=f"{tag}_bev")

            emit_proj(P, f"{tag}p{h}_{which}", lnt, None, 48, ev,
                      stash=wqkvt, oc_list=[oc])
        qt, kt, vt = qk[0], qk[1], ctx[h]
        rk = [(f"{tag}r{h}", i) for i in range(4)]
        for which, c0k in ((0, 0), (1, NPC)):
            P.op("dve", lambda qt=qt, kt=kt, c0k=c0k:
                 nc.vector.tensor_mul(m0.ap(), qt.ap(c0=NPC, c1=TPC),
                                      kt.ap(c0=c0k, c1=c0k + NPC)),
                 reads=[qt, kt], writes=[m0], tag=f"{tag}_dot")
            for f in range(2):
                q_ = which * 2 + f
                P.op("pe", lambda q_=q_, f=f, onesc=onesc:
                     nc.tensor.matmul(P0[0:1, q_ * FS:(q_ + 1) * FS], onesc,
                                      m0.ap(c0=f * FS, c1=(f + 1) * FS),
                                      start=True, stop=True),
                     reads=[m0, P.ckey("ones_col")], writes=[("P0", q_)],
                     tag=f"{tag}_dsum")
            P.op("dve", lambda which=which:
                 nc.vector.tensor_scalar_mul(
                     rowap(which), P0[0:1, which * 1024:(which + 1) * 1024],
                     SC),
                 reads=[("P0", 2 * which), ("P0", 2 * which + 1)],
                 writes=[rk[which], rows], tag=f"{tag}_sev")
        P.op("dve", lambda: nc.vector.tensor_max(rowap(2), rowap(0), rowap(1)),
             reads=[rk[0], rk[1]], writes=[rk[2]], tag=f"{tag}_max")
        for i in range(2):
            P.op("dve", lambda i=i: nc.vector.tensor_sub(rowap(i), rowap(i),
                                                         rowap(2)),
                 reads=[rk[i], rk[2]], writes=[rk[i]], tag=f"{tag}_sub")
            P.op("act", lambda i=i: nc.scalar.activation(rowap(i), rowap(i),
                                                         AF.Exp),
                 reads=[rk[i]], writes=[rk[i]], tag=f"{tag}_exp")
        P.op("dve", lambda: nc.vector.tensor_add(rowap(2), rowap(0), rowap(1)),
             reads=[rk[0], rk[1]], writes=[rk[2]], tag=f"{tag}_den")
        P.op("dve", lambda: nc.vector.reciprocal(rowap(3), rowap(2)),
             reads=[rk[2]], writes=[rk[3]], tag=f"{tag}_rec")
        for i in range(2):
            P.op("dve", lambda i=i: nc.vector.tensor_mul(rowap(i), rowap(i),
                                                         rowap(3)),
                 reads=[rk[i], rk[3]], writes=[rk[i]], tag=f"{tag}_p")
            P.op("dve", lambda i=i: nc.vector.tensor_copy(
                     prow.ap(rows=1, c0=i * 1024, c1=(i + 1) * 1024),
                     rowap(i)),
                 reads=[rk[i]], writes=[prow], tag=f"{tag}_pb")
        for f in range(4):
            P.op("pe", lambda f=f, onesr=onesr: nc.tensor.matmul(
                     P0[:, f * FS:(f + 1) * FS], onesr,
                     prow.ap(rows=1, c0=f * FS, c1=(f + 1) * FS),
                     start=True, stop=True),
                 reads=[prow, P.ckey("ones_row")], writes=[("P0", f)],
                 tag=f"{tag}_pbc")
        P.op("dve", lambda vt=vt: nc.vector.tensor_mul(
                 m0.ap(), P0[:, 0:1024], vt.ap(c0=0, c1=NPC)),
             reads=[("P0", 0), ("P0", 1), vt], writes=[m0], tag=f"{tag}_pv0")
        P.op("dve", lambda vt=vt: nc.vector.tensor_mul(
                 vt.ap(c0=NPC, c1=TPC), P0[:, 1024:2048],
                 vt.ap(c0=NPC, c1=TPC)),
             reads=[("P0", 2), ("P0", 3), vt], writes=[vt], tag=f"{tag}_pv1")
        P.op("dve", lambda vt=vt: nc.vector.tensor_add(
                 vt.ap(c0=NPC, c1=TPC), vt.ap(c0=NPC, c1=TPC), m0.ap()),
             reads=[vt, m0], writes=[vt], tag=f"{tag}_cadd")
    for t in qk + [m0, rows, prow]:
        t.free()
    for t in lnt:
        t.free()

    xres = [P.tile_b(FS, f"{tag}_xr") for _ in range(2)]
    ostg = [P.tile_b(FS, f"{tag}_os") for _ in range(2)]

    def dense_evac(oc, f, pap, pkey, par):
        xr, os_ = xres[par], ostg[par]
        fo = f * FS
        P.dma("qact", lambda xr=xr, oc=oc, fo=fo: nc.scalar.dma_start(
                  out=xr.ap(), in_=xr_ap(xin, oc, fo, fo + FS)),
              reads=[xr_key(xin, oc)], writes=[xr], tag=f"{tag}_lre")
        P.op("dve", lambda os_=os_, xr=xr, pap=pap:
             nc.vector.scalar_tensor_tensor(os_.ap(), pap, 1.0, xr.ap(),
                                            ALU.mult, ALU.add),
             reads=[pkey, xr], writes=[os_], tag=f"{tag}_res")
        P.op("dve", lambda os_=os_, oc=oc: nc.vector.tensor_scalar(
                 os_.ap(), os_.ap(), bden.ap(c0=oc, c1=oc + 1), None,
                 ALU.add),
             reads=[os_, bden], writes=[os_], tag=f"{tag}_bd")
        P.dma("qact", lambda os_=os_, oc=oc, fo=fo: nc.scalar.dma_start(
                  out=xout.ap()[oc * 128:(oc + 1) * 128, fo:fo + FS],
                  in_=os_.ap()),
              reads=[os_], writes=[xr_key(xout, oc)], tag=f"{tag}_stx")

    emit_proj(P, tag + "d", ctx, None, FC, dense_evac, stash=wdenset)
    for t in ctx + xres + ostg + [bqkv, bden]:
        t.free()


def emit_input_prep(L, x_tm, x0):
    P = L.P
    nc = P.nc
    xi = P.dram_in["x_in"]
    P.dma("qpool", lambda: nc.gpsimd.dma_start(
              out=x_tm.ap(), in_=xi.rearrange("b n d -> (b n) d")),
          writes=[("xtm",)], tag="prep_cast")
    t = [P.tile_b(TPC, "prep_t") for _ in range(2)]
    for ic in range(FC):
        tt = t[ic % 2]
        P.dma("qsp", lambda ic=ic, tt=tt: nc.sync.dma_start(
                  out=tt.ap(), in_=x_tm.ap()[:, ic * 128:(ic + 1) * 128],
                  transpose=True),
              reads=[("xtm",)], writes=[tt], tag=f"prep_tr{ic}")
        P.dma("qact", lambda ic=ic, tt=tt: nc.scalar.dma_start(
                  out=xr_ap(x0, ic), in_=tt.ap()),
              reads=[tt], writes=[xr_key(x0, ic)], tag=f"prep_st{ic}")
    t[0].free()
    t[1].free()


def emit_output(L, xin, out):
    P = L.P
    nc = P.nc
    tb = [P.tile_b(D, "out_tb") for _ in range(2)]
    tf = [P.tile_f(D, "out_tf") for _ in range(2)]
    for tc in range(TPC // 128):
        b_ = tb[tc % 2]
        f_ = tf[tc % 2]
        P.dma("qsp", lambda tc=tc, b_=b_: nc.sync.dma_start(
                  out=b_.ap(), in_=xin.ap()[:, tc * 128:(tc + 1) * 128],
                  transpose=True),
              reads=[xr_key(xin, ic2) for ic2 in range(FC)],
              writes=[b_], tag=f"out_tr{tc}")
        P.op("dve", lambda b_=b_, f_=f_: nc.vector.tensor_copy(f_.ap(), b_.ap()),
             reads=[b_], writes=[f_], tag="out_cp")
        P.dma("qact", lambda tc=tc, f_=f_: nc.scalar.dma_start(
                  out=out.ap()[tc * 128:(tc + 1) * 128, :], in_=f_.ap()),
              reads=[f_], writes=[("OUT",)], tag=f"out_st{tc}")
    for x in tb + tf:
        x.free()
    P.out_queues.add("qact")


class Launch:
    def __init__(self, name):
        self.P = Prog(name)
        self.name = name

    def inp(self, name, shape, dtype=F32):
        h = self.P.nc.dram_tensor(name, list(shape), dtype,
                                  kind="ExternalInput")
        self.P.dram_in[name] = h.ap()
        return h

    def outp(self, name, shape, dtype=F32):
        h = self.P.nc.dram_tensor(name, list(shape), dtype,
                                  kind="ExternalOutput")
        self.P.dram_out[name] = h
        return h

    def internal(self, name, shape, dtype=BF16):
        return self.P.nc.dram_tensor(name, list(shape), dtype, kind="Internal")


def _build_launch1(sc):
    L = Launch("l1")
    P = L.P
    L.inp("x_in", [B, NPC, D], F32)
    L.inp("L0_ffn_W1", [DFF, D], F32)
    L.inp("L0_ffn_W2", [D, DFF], F32)
    L.inp("L0_ffn_W3", [DFF, D], F32)
    for w in ("Wq", "Wk", "Wv"):
        L.inp(f"L0_attn_{w}", [D, D], F32)
    L.inp("L0_attn_Wg1", [D, 128], F32)
    L.inp("rope_cc", [128, TPC], BF16)
    L.inp("rope_ss", [128, TPC], BF16)
    x_tm = L.internal("x_tm", [TPC, D], BF16)
    x0 = L.internal("x0", [FC * 128, TPC], BF16)
    x1 = L.outp("x1", [FC * 128, TPC], BF16)
    hid = L.internal("hid", [HC * 128, TPC], BF16)
    w1n = L.internal("w1n", [DFF, D], BF16)
    w3n = L.internal("w3n", [DFF, D], BF16)
    w2n = L.internal("w2n", [D, DFF], BF16)
    w1t = L.internal("w1t", [D, DFF], BF16)
    w3t = L.internal("w3t", [D, DFF], BF16)
    w2t = L.internal("w2t", [DFF, D], BF16)
    qsp = L.outp("qsp", [HEADS * 128, TPC], BF16)
    ksp = L.outp("ksp", [HEADS * 128, TPC], BF16)
    vtsp = L.outp("vtsp", [HEADS * 2 * JPC * 128, CK], BF16)
    kvsp = L.outp("kvsp", [HEADS * 2 * JPC * 128, CK], F32)
    s_out = L.outp("s_out", [2 * HEADS * 128, CK], F32)
    gtmp = L.outp("gtmp", [128, TPC], BF16)

    P.alloc_arenas(78000, 12288)
    P.alloc_psum()
    P.load_consts(sc, ["ones_col", "ones_row", "h128", "swap", "ident", "dk"],
                  extra=("rope_cc", "rope_ss"))

    emit_input_prep(L, x_tm, x0)
    emit_weight_stash(P, "L0_ffn_W1", w1t, w1n, "w1t")
    emit_weight_stash(P, "L0_ffn_W3", w3t, w3n, "w3t")
    emit_weight_stash(P, "L0_ffn_W2", w2t, w2n, "w2t", fwht=True)
    emit_ffn(P, "f0", x0, x1, w1t, w3t, w2t, hid, scale=0.5)
    emit_attn0_phaseA(P, x1, qsp, ksp, vtsp, kvsp, s_out, gtmp)
    P.out_queues.add("qact")
    P.finalize()
    return L


def _build_launch2(sc, upto=6):
    L = Launch("l2")
    P = L.P
    x1 = L.inp("x1", [FC * 128, TPC], BF16)
    qsp = L.inp("qsp", [HEADS * 128, TPC], BF16)
    ksp = L.inp("ksp", [HEADS * 128, TPC], BF16)
    vtsp = L.inp("vtsp", [HEADS * 2 * JPC * 128, CK], BF16)
    kvsp = L.inp("kvsp", [HEADS * 2 * JPC * 128, CK], F32)
    s_all = L.inp("s_all", [NCORE * 2 * HEADS * 128, CK], F32)
    gtmp = L.inp("gtmp", [128, TPC], BF16)
    L.inp("sprefw", [128, NCORE * HEADS], F32)
    L.inp("L0_attn_Wg2", [128, D], F32)
    L.inp("L0_attn_norm_weight", [D], F32)
    L.inp("L0_attn_Wo", [D, D], F32)
    L.inp("L0_ffn_W1", [DFF, D], F32)
    L.inp("L0_ffn_W2", [D, DFF], F32)
    L.inp("L0_ffn_W3", [DFF, D], F32)
    L.inp("L1_ffn_w_gate", [DFF, D], F32)
    L.inp("L1_ffn_w_up", [DFF, D], F32)
    L.inp("L1_ffn_w_down", [D, DFF], F32)
    L.inp("L1_attn_w_qkv", [3 * D, D], F32)
    L.inp("L1_attn_b_qkv", [3 * D], F32)
    L.inp("L1_attn_w_dense", [D, D], F32)
    L.inp("L1_attn_b_dense", [D], F32)

    x2 = L.internal("x2", [FC * 128, TPC], BF16)
    x3 = L.internal("x3", [FC * 128, TPC], BF16)
    x4 = L.internal("x4", [FC * 128, TPC], BF16)
    x5 = L.internal("x5", [FC * 128, TPC], BF16)
    x6 = L.internal("x6", [FC * 128, TPC], BF16)
    hid = L.internal("hid", [HC * 128, TPC], BF16)
    out = L.outp("out", [TPC, D], F32)
    if DEBUG_ATTN0:
        L.outp("od1", [HEADS * 128, TPC], BF16)
        L.outp("od2", [HEADS * 128, TPC], BF16)
    stash_defs = [
        ("L0_ffn_W1", "w1n", "w1t", [DFF, D], False),
        ("L0_ffn_W3", "w3n", "w3t", [DFF, D], False),
        ("L0_ffn_W2", "w2n", "w2t", [D, DFF], True),
        ("L1_ffn_w_gate", "wgn", "wgt", [DFF, D], False),
        ("L1_ffn_w_up", "wun", "wut", [DFF, D], False),
        ("L1_ffn_w_down", "wdn", "wdt", [D, DFF], False),
        ("L1_attn_w_qkv", "wqn", "wqt", [3 * D, D], False),
        ("L1_attn_w_dense", "wsn", "wst", [D, D], False),
    ]
    stashes = {}
    for nm, natn, stn, shp, fw in stash_defs:
        nat = L.internal(natn, shp, BF16)
        st = L.internal(stn, [shp[1], shp[0]], BF16)
        stashes[stn] = (nm, nat, st, fw)

    P.alloc_arenas(80000, 12288)
    P.alloc_psum()
    P.load_consts(sc, ["ones_col", "ones_row", "h128", "dmask_t", "dq"],
                  extra=("sprefw",))

    chain = [x1, x2, x3, x4, x5, x6]
    xo = chain[min(upto, 6) - 1]

    emit_attn0_phaseB(P, x1, x2, qsp, ksp, vtsp, kvsp, s_all, gtmp)
    P.free_consts(["dmask_t", "dq"])
    if upto >= 3:
        for stn in ("w1t", "w3t", "w2t"):
            nm, nat, st, fw = stashes[stn]
            emit_weight_stash(P, nm, st, nat, stn, fwht=fw)
        emit_ffn(P, "f2", x2, x3, stashes["w1t"][2], stashes["w3t"][2],
                 stashes["w2t"][2], hid, scale=0.5)
    if upto >= 4:
        for stn in ("wgt", "wut", "wdt"):
            nm, nat, st, fw = stashes[stn]
            emit_weight_stash(P, nm, st, nat, stn, fwht=fw)
        emit_ffn(P, "f3", x3, x4, stashes["wgt"][2], stashes["wut"][2],
                 stashes["wdt"][2], hid, scale=0.5)
    if upto >= 5:
        for stn in ("wqt", "wst"):
            nm, nat, st, fw = stashes[stn]
            emit_weight_stash(P, nm, st, nat, stn, fwht=fw)
        emit_attn1(P, x4, x5, stashes["wqt"][2], stashes["wst"][2])
    if upto >= 6:
        emit_ffn(P, "f5", x5, x6, stashes["wgt"][2], stashes["wut"][2],
                 stashes["wdt"][2], hid, scale=0.5)
    emit_output(L, xo, out)
    P.finalize()
    return L


def kernel(**inputs):
    sc = shared_consts()
    x = np.ascontiguousarray(inputs["L0_attn_x"])
    hc = [host_consts(c) for c in range(NCORE)]

    L1 = _build_launch1(sc)
    in_maps = []
    for c in range(NCORE):
        in_maps.append({
            "x_in": np.ascontiguousarray(x[:, c * NPC:(c + 1) * NPC, :]),
            "L0_ffn_W1": inputs["L0_ffn_W1"],
            "L0_ffn_W2": inputs["L0_ffn_W2"],
            "L0_ffn_W3": inputs["L0_ffn_W3"],
            "L0_attn_Wq": inputs["L0_attn_Wq"],
            "L0_attn_Wk": inputs["L0_attn_Wk"],
            "L0_attn_Wv": inputs["L0_attn_Wv"],
            "L0_attn_Wg1": inputs["L0_attn_Wg1"],
            "rope_cc": hc[c]["rope_cc"],
            "rope_ss": hc[c]["rope_ss"],
        })
    res1 = run_bass_kernel_spmd(L1.P.nc, in_maps, core_ids=list(range(NCORE)))
    L1.P.close()
    r1 = res1.results

    s_all = np.concatenate([np.asarray(r1[c]["s_out"]) for c in range(NCORE)],
                           axis=0)
    L2 = _build_launch2(sc, upto=UPTO)
    in_maps2 = []
    for c in range(NCORE):
        in_maps2.append({
            "x1": np.asarray(r1[c]["x1"]),
            "qsp": np.asarray(r1[c]["qsp"]),
            "ksp": np.asarray(r1[c]["ksp"]),
            "vtsp": np.asarray(r1[c]["vtsp"]),
            "kvsp": np.asarray(r1[c]["kvsp"]),
            "s_all": s_all,
            "gtmp": np.asarray(r1[c]["gtmp"]),
            "sprefw": hc[c]["sprefw"],
            "L0_attn_Wg2": inputs["L0_attn_Wg2"],
            "L0_attn_norm_weight": np.ascontiguousarray(
                inputs["L0_attn_norm_weight"]),
            "L0_attn_Wo": inputs["L0_attn_Wo"],
            "L0_ffn_W1": inputs["L0_ffn_W1"],
            "L0_ffn_W2": inputs["L0_ffn_W2"],
            "L0_ffn_W3": inputs["L0_ffn_W3"],
            "L1_ffn_w_gate": inputs["L1_ffn_w_gate"],
            "L1_ffn_w_up": inputs["L1_ffn_w_up"],
            "L1_ffn_w_down": inputs["L1_ffn_w_down"],
            "L1_attn_w_qkv": inputs["L1_attn_w_qkv"],
            "L1_attn_b_qkv": np.ascontiguousarray(inputs["L1_attn_b_qkv"]),
            "L1_attn_w_dense": inputs["L1_attn_w_dense"],
            "L1_attn_b_dense": np.ascontiguousarray(
                inputs["L1_attn_b_dense"]),
        })
    res2 = run_bass_kernel_spmd(L2.P.nc, in_maps2, core_ids=list(range(NCORE)))
    L2.P.close()

    out = np.empty((B, N, D), dtype=np.float32)
    for c in range(NCORE):
        tm = np.asarray(res2.results[c]["out"])
        out[0, c * NPC:(c + 1) * NPC, :] = tm[0:NPC]
        out[1, c * NPC:(c + 1) * NPC, :] = tm[NPC:]
    return out
